# version 04 (final)
# baseline (speedup 1.0000x reference)
import contextlib
import numpy as np
import concourse.bass as bass
import concourse.mybir as mybir
from concourse.bass_utils import run_bass_kernel_spmd

F32 = mybir.dt.float32
BF16 = mybir.dt.bfloat16
AF = mybir.ActivationFunctionType
ALU = mybir.AluOpType
AX = mybir.AxisListType

NC = 8
T = 1024
NCH = 8
CH = 128
D = 4096
GRP = 8
HPG = 16
P = 64
N = 128
INTER = 8192
CONVD = 10240
H = 128
EPS = 1e-5
LN_EPS = 1e-5
DT_MAX = 100.0
NEG = -1e30

NFT = 19
KD = 32
TW = 1040
COL0 = 16

ENGINES = ("tensor", "vector", "scalar", "gpsimd", "sync")
GROUPS = list(range(GRP))
SSD_PARTS = ("P1", "CC", "P2")



def _ftile_rows(g, j):
    if j < 8:
        base = INTER + 1024 * g + 128 * j
        return np.arange(base, base + 128), 0
    if j == 8:
        base = INTER + INTER + 128 * g
        return np.arange(base, base + 128), 0
    if j == 9:
        base = INTER + INTER + 1024 + 128 * g
        return np.arange(base, base + 128), 0
    if j < 18:
        base = 1024 * g + 128 * (j - 10)
        return np.arange(base, base + 128), 0
    base = INTER + CONVD + HPG * g
    return np.arange(base, base + HPG), 112


def _conv_ch(g, j):
    if j < 8:
        base = 1024 * g + 128 * j
    elif j == 8:
        base = INTER + 128 * g
    else:
        base = INTER + 1024 + 128 * g
    return np.arange(base, base + 128)


def host_prepare(L0_attn_x, L0_attn_in_proj_w, L0_attn_conv_w, L0_attn_conv_b,
                 L0_attn_dt_bias, L0_attn_A_log, L0_attn_D, L0_attn_norm_w,
                 L0_attn_out_proj_w, L0_ffn_gate_w, L0_ffn_base_w, L0_ffn_base_b,
                 L0_ffn_lora_A, L0_ffn_lora_B):
    x = np.ascontiguousarray(np.asarray(L0_attn_x, dtype=np.float32).reshape(NC * T, D))
    W = np.asarray(L0_attn_in_proj_w, dtype=np.float32)

    w_in = np.zeros((GRP, NFT, 128, KD * 128), dtype=np.float32)
    for g in range(GRP):
        for j in range(NFT):
            rows, pad = _ftile_rows(g, j)
            blk = W[rows]
            if pad:
                blk = np.concatenate([blk, np.zeros((pad, D), np.float32)], 0)
            t = blk.T.reshape(KD, 128, 128).transpose(1, 0, 2)
            w_in[g, j] = t.reshape(128, KD * 128)

    cw = np.asarray(L0_attn_conv_w, dtype=np.float32)[:, 0, :]
    cb = np.asarray(L0_attn_conv_b, dtype=np.float32)
    conv_w_arr = np.zeros((GRP, 128, 40), np.float32)
    conv_b_arr = np.zeros((GRP, 128, 10), np.float32)
    for g in range(GRP):
        for j in range(10):
            ch = _conv_ch(g, j)
            conv_w_arr[g, :, 4 * j:4 * (j + 1)] = cw[ch]
            conv_b_arr[g, :, j] = cb[ch]

    Wout = np.asarray(L0_attn_out_proj_w, dtype=np.float32)
    wt = Wout.T.reshape(64, 128, D)
    w_out_arr = np.ascontiguousarray(
        np.stack([wt[:, :, ob * 1024:(ob + 1) * 1024] for ob in range(4)], 0))

    base_w = np.asarray(L0_ffn_base_w, dtype=np.float32)
    w_base_arr = np.ascontiguousarray(base_w.T.reshape(KD, 128, D))

    gate_w = np.asarray(L0_ffn_gate_w, dtype=np.float32)
    lora_A = np.asarray(L0_ffn_lora_A, dtype=np.float32)
    gl = np.concatenate([gate_w, lora_A.reshape(128, D)], 0)
    w_gl_arr = np.ascontiguousarray(gl.T.reshape(KD, 128, 136))

    lora_B = np.asarray(L0_ffn_lora_B, dtype=np.float32)
    w_lb_arr = np.ascontiguousarray(lora_B.transpose(0, 2, 1).reshape(128, D))

    sidx = np.arange(128)
    maskblk = np.where(sidx[:, None] > sidx[None, :], np.float32(NEG), 0.0).astype(np.float32)
    shared = dict(
        w_in=w_in, conv_w_arr=conv_w_arr, conv_b_arr=conv_b_arr,
        w_out_arr=w_out_arr, w_base_arr=w_base_arr, w_gl_arr=w_gl_arr, w_lb_arr=w_lb_arr,
        base_b=np.asarray(L0_ffn_base_b, np.float32).reshape(1, D),
        norm_w=np.asarray(L0_attn_norm_w, np.float32).reshape(1, INTER),
        a_log=np.asarray(L0_attn_A_log, np.float32).reshape(1, H),
        d_row=np.asarray(L0_attn_D, np.float32).reshape(1, H),
        dtb_row=np.asarray(L0_attn_dt_bias, np.float32).reshape(1, H),
        uincl=(sidx[:, None] <= sidx[None, :]).astype(np.float32),
        ustrict=(sidx[:, None] > sidx[None, :]).astype(np.float32),
        maskblk4=np.tile(maskblk, (1, 4)),
        g0mask=(sidx[:, None] <= sidx[None, :]).astype(np.float32),
        ident=np.eye(128, dtype=np.float32),
        onescol=np.ones((128, 1), np.float32),
        onesrow=np.concatenate([np.ones((1, 128), np.float32),
                                np.zeros((31, 128), np.float32),
                                np.ones((1, 128), np.float32)], 0),
    )

    in_maps = []
    for i in range(NC):
        m = dict(shared)
        m["xs"] = np.ascontiguousarray(x[T * i:T * (i + 1)])
        m["xprev"] = (np.zeros((3, D), np.float32) if i == 0
                      else np.ascontiguousarray(x[T * i - 3:T * i]))
        kidx = np.arange(8)
        jidx = np.arange(8)
        mi = ((jidx[None, :] < kidx[:, None]) & (kidx[:, None] < i)).astype(np.float32)
        m["mi"] = np.ascontiguousarray(mi)
        m["maskcol_i"] = np.where(jidx < i, 0.0, np.float32(NEG)).astype(np.float32).reshape(8, 1)
        in_maps.append(m)
    return in_maps



class Sem:
    __slots__ = ("name", "h", "count")

    def __init__(self, name):
        self.name = name
        self.h = None
        self.count = 0


class Plan:
    def __init__(self):
        self.ops = {e: [] for e in ENGINES}
        self.sems = []

    def sem(self, name):
        sm = Sem(name)
        self.sems.append(sm)
        return sm

    def emit(self, eng_obj, eng_name):
        dedup = eng_name in ("tensor", "vector", "scalar")
        seen = {}
        for fn, waits, inc, inc_n in self.ops[eng_name]:
            for sm, v in waits:
                if not dedup or seen.get(id(sm), -1) < v:
                    eng_obj.wait_ge(sm.h, v)
                    seen[id(sm)] = v
            inst = fn(eng_obj)
            if inc is not None:
                assert inst is not None, f"op returned None but needs inc ({eng_name})"
                inst.then_inc(inc.h, inc_n)

    def n_ops(self):
        return {e: len(v) for e, v in self.ops.items()}


class Arena:

    def __init__(self, handle, n):
        self.h = handle
        self.n = n
        self.ptr = 0
        self.marks = []
        self.peak = 0

    def _take(self, elems, align=64):
        off = (self.ptr + align - 1) // align * align
        self.ptr = off + elems
        self.peak = max(self.peak, self.ptr)
        assert self.ptr <= self.n, f"arena overflow {self.ptr} > {self.n}"
        return off

    def bf(self, w):
        off = self._take(w)
        return self.h[:, off:off + w]

    def f32(self, w):
        off = self._take(2 * w)
        return self.h[:, off:off + 2 * w].bitcast(F32)

    def mark(self):
        self.marks.append(self.ptr)

    def release(self):
        self.marks and None
        self.ptr = self.marks.pop()


class Ctx:

    def __init__(self):
        self.nc = None
        self.ap = {}
        self.plan = Plan()
        self.s = {}
        self.A = None
        self.pbank = None
        self.pdub = None
        self.c = {}
        self.t = {}

    def op(self, eng, waits=(), inc=None, inc_n=1):
        if inc is None and eng == "vector":
            inc = self.s["dve"]
        elif inc is None and eng == "scalar":
            inc = self.s["act"]

        def deco(fn):
            self.plan.ops[eng].append((fn, tuple(waits), inc, inc_n))
            if inc is not None:
                inc.count += inc_n
            return None
        return deco



def ln_cols(ctx, eng_pref, xin, rows, width, stats, eps, wait):
    wait = list(wait)
    """LN stats for xin [rows x width] f32. stats cols: 0 sum, 1 sumsq, 2 mean,
    3 var, 4 rstd, 5 mean^2, 6 ln. Returns act-sem milestone when rstd ready."""
    s = ctx.s
    sq = ctx.t["ln_sq"]

    @ctx.op("scalar", waits=wait)
    def _(e, xin=xin, rows=rows, sq=sq, width=width):
        return e.activation(out=sq[0:rows, 0:width], in_=xin, func=AF.Square)
    m_sq = s["act"].count

    @ctx.op("vector", waits=wait + [(s["act"], m_sq)])
    def _(e, xin=xin, rows=rows, stats=stats, sq=sq, width=width):
        e.reduce_sum(out=stats[0:rows, 0:1], in_=xin, axis=AX.X)
        return e.reduce_sum(out=stats[0:rows, 1:2], in_=sq[0:rows, 0:width], axis=AX.X)
    m_red = s["dve"].count

    @ctx.op("vector", waits=[(s["dve"], m_red)])
    def _(e, stats=stats, rows=rows, width=width):
        e.tensor_scalar(out=stats[0:rows, 2:3], in0=stats[0:rows, 0:1],
                        scalar1=1.0 / width, scalar2=None, op0=ALU.mult)
        return e.tensor_scalar(out=stats[0:rows, 3:4], in0=stats[0:rows, 1:2],
                               scalar1=1.0 / width, scalar2=None, op0=ALU.mult)
    m_sc = s["dve"].count

    @ctx.op("scalar", waits=[(s["dve"], m_sc)])
    def _(e, stats=stats, rows=rows):
        return e.activation(out=stats[0:rows, 5:6], in_=stats[0:rows, 2:3], func=AF.Square)
    m_msq = s["act"].count

    @ctx.op("vector", waits=[(s["act"], m_msq)])
    def _(e, stats=stats, rows=rows):
        return e.tensor_sub(stats[0:rows, 3:4], stats[0:rows, 3:4], stats[0:rows, 5:6])
    m_var = s["dve"].count

    @ctx.op("scalar", waits=[(s["dve"], m_var)])
    def _(e, stats=stats, rows=rows):
        return e.activation(out=stats[0:rows, 6:7], in_=stats[0:rows, 3:4], func=AF.Ln,
                            bias=ctx.t["eps_col"][0:rows, 0:1])
    m_ln = s["act"].count

    @ctx.op("scalar", waits=[(s["act"], m_ln)])
    def _(e, stats=stats, rows=rows):
        return e.activation(out=stats[0:rows, 4:5], in_=stats[0:rows, 6:7], func=AF.Exp,
                            scale=-0.5)
    return s["act"].count



def build_program(phases=("H",), dbg=()):
    nc = bass.Bass("TRN2", target_bir_lowering=False)
    ctx = Ctx()
    ctx.nc = nc
    ap = ctx.ap

    def din(name, shape, dt=F32):
        ap[name] = nc.dram_tensor(name, shape, dt, kind="ExternalInput").ap()

    def dout(name, shape, dt=F32):
        ap[name] = nc.dram_tensor(name, shape, dt, kind="ExternalOutput").ap()

    def dram(name, shape, dt):
        ap[name] = nc.dram_tensor(name, shape, dt).ap()

    din("xs", [T, D]); din("xprev", [3, D])
    din("w_in", [GRP, NFT, 128, KD * 128])
    din("conv_w_arr", [GRP, 128, 40]); din("conv_b_arr", [GRP, 128, 10])
    din("w_out_arr", [4, 64, 128, 1024]); din("w_base_arr", [KD, 128, D])
    din("w_gl_arr", [KD, 128, 136]); din("w_lb_arr", [128, D])
    din("base_b", [1, D]); din("norm_w", [1, INTER]); din("a_log", [1, H])
    din("d_row", [1, H]); din("dtb_row", [1, H])
    din("uincl", [128, 128]); din("ustrict", [128, 128]); din("maskblk4", [128, 512])
    din("g0mask", [128, 128]); din("ident", [128, 128])
    din("onescol", [128, 1]); din("onesrow", [33, 128])
    din("mi", [8, 8]); din("maskcol_i", [8, 1])

    dout("ys", [T, D])
    for name, shape, dt in dbg:
        dout(name, shape, dt)

    dram("acum_hilo", [2, HPG, T], BF16)
    dram("alast8", [1, 128], F32)
    dram("coef_b", [1, 128], F32)
    dram("sinloc_d", [NCH, 128, HPG * P], F32)
    dram("hstm_d", [NCH, 128, HPG * P], BF16)
    dram("gg_fm_d", [64, NCH, 128, 128], BF16)
    dram("ffn_d", [NCH, 128, D], BF16)
    dram("cc_in", [129, HPG * P], F32)
    ap["cc_out"] = nc.dram_tensor("cc_out", [NC * 129, HPG * P], F32,
                                  addr_space="Shared").ap()

    plan = ctx.plan
    sem_names = ("mm", "dve", "act", "pool",
                 "wconst", "wx", "ww0", "ww1", "wsp",
                 "wtrh", "wtra", "wtrb",
                 "wspa", "wspb", "wrda", "wrdb",
                 "wr2a", "wr2b", "wrha", "wrhb", "wsl",
                 "wgga", "wggb", "wacum", "wout", "cc")
    ctx.s = {n: plan.sem(n) for n in sem_names}

    with contextlib.ExitStack() as stack:
        cN = {}
        for nm, shape, dt in [
            ("uincl", [128, 128], F32), ("ustrict", [128, 128], F32),
            ("maskblk", [128, 512], BF16), ("g0mask", [128, 128], BF16),
            ("identf", [128, 128], F32), ("identb", [128, 128], BF16),
            ("onescol", [128, 1], F32), ("onesrow", [33, 128], BF16),
            ("onesrow_f", [33, 128], F32), ("mi", [8, 8], F32), ("maskcol", [8, 1], F32),
            ("baseb", [1, D], BF16), ("expalog", [128, 128], F32), ("alogbc", [128, 128], F32),
            ("dbc", [128, 128], BF16), ("dtbbc", [128, 128], F32),
            ("eps_col", [128, 1], F32), ("epsln_col", [128, 1], F32),
        ]:
            cN[nm] = stack.enter_context(nc.sbuf_tensor("c_" + nm, shape, dt))
        ctx.c = cN
        ctx.t["eps_col"] = cN["epsln_col"]
        arena_elems = (nc.sbuf_bytes_remaining - 2048) // 2 // 64 * 64
        ar = stack.enter_context(nc.sbuf_tensor("arena", [128, arena_elems], BF16))

        pd = [stack.enter_context(nc.psum_tensor(f"pd{i}", [128, 1024], F32))
              for i in range(4)]
        ctx.pdub = pd
        ctx.pbank = [pd[i // 2][:, 512 * (i % 2):512 * (i % 2 + 1)] for i in range(8)]

        for sm in plan.sems:
            sm.h = stack.enter_context(nc.semaphore(sm.name))

        with nc.Block() as block:
            A = Arena(ar, arena_elems)
            ctx.A = A
            ctx.t["ln_sq"] = A.f32(D)
            plan_const_loads(ctx)
            A.mark()
            if "H" in phases:
                plan_phase_h(ctx)
            if "G" in phases:
                plan_groups(ctx, phases)
            A.release()
            if "OP" in phases:
                plan_outproj(ctx)
            if "X2" in phases:
                plan_x2(ctx)
            if "FFN" in phases:
                plan_ffn(ctx)
            if "X3" in phases:
                plan_x3(ctx)
            if dbg:
                plan_post_dbg(ctx)
            plan_debug(ctx, dbg)
            plan_finish(ctx)

            @block.gpsimd
            def _(e):
                plan.emit(e, "gpsimd")

            @block.sync
            def _(e):
                plan.emit(e, "sync")

            @block.tensor
            def _(e):
                plan.emit(e, "tensor")

            @block.vector
            def _(e):
                plan.emit(e, "vector")

            @block.scalar
            def _(e):
                plan.emit(e, "scalar")

    return nc


def plan_const_loads(ctx):
    s, ap, c = ctx.s, ctx.ap, ctx.c
    loads = [
        (c["uincl"][:], ap["uincl"]), (c["ustrict"][:], ap["ustrict"]),
        (c["maskblk"][:], ap["maskblk4"]), (c["g0mask"][:], ap["g0mask"]),
        (c["identf"][:], ap["ident"]), (c["identb"][:], ap["ident"]),
        (c["onescol"][:], ap["onescol"]), (c["onesrow"][:], ap["onesrow"]),
        (c["onesrow_f"][:], ap["onesrow"]),
        (c["mi"][:], ap["mi"]), (c["maskcol"][:], ap["maskcol_i"]),
        (c["baseb"][:], ap["base_b"]),
        (c["alogbc"][:], ap["a_log"][0:1, :].partition_broadcast(128)),
        (c["dbc"][:], ap["d_row"][0:1, :].partition_broadcast(128)),
        (c["dtbbc"][:], ap["dtb_row"][0:1, :].partition_broadcast(128)),
    ]
    for dst, src in loads:
        @ctx.op("gpsimd", inc=s["wconst"], inc_n=16)
        def _(e, dst=dst, src=src):
            return e.dma_start(out=dst, in_=src)
    m = s["wconst"].count
    ctx.m_const = m

    @ctx.op("scalar", waits=[(s["wconst"], m)])
    def _(e, c=c):
        e.activation(out=c["expalog"][:], in_=c["alogbc"][:], func=AF.Exp)
        return e.activation(out=c["epsln_col"][:], in_=c["onescol"][:], func=AF.Copy, scale=LN_EPS)
    ctx.m_const_act = s["act"].count



def plan_phase_h(ctx):
    s, ap, A, c = ctx.s, ctx.ap, ctx.A, ctx.c

    h_fm = A.bf(KD * TW)
    ctx.t["h_fm"] = h_fm
    ctx.dbg_src = {}

    A.mark()
    xt = A.f32(D)
    stats = A.f32(8)
    hbf = A.bf(D)
    pd = ctx.pdub

    @ctx.op("vector")
    def _(e, h_fm=h_fm):
        return e.memset(h_fm, 0.0)
    m_h = ctx.s["dve"].count
    m_tr_prev = 0
    for cidx in range(-1, NCH):
        rows = 3 if cidx < 0 else 128
        src = ap["xprev"] if cidx < 0 else ap["xs"][CH * cidx:CH * (cidx + 1), :]

        @ctx.op("gpsimd", waits=[(s["dve"], m_h)], inc=s["wx"], inc_n=16)
        def _(e, src=src, rows=rows, xt=xt):
            return e.dma_start(out=xt[0:rows, :], in_=src)
        wld = s["wx"].count

        m_rstd = ln_cols(ctx, None, xt[0:rows, :], rows, D, stats, LN_EPS,
                         [(s["wx"], wld)])

        @ctx.op("vector", waits=[(s["act"], m_rstd), (s["wtrh"], m_tr_prev), (s["mm"], s["mm"].count)])
        def _(e, xt=xt, hbf=hbf, stats=stats, rows=rows):
            return e.tensor_scalar(out=hbf[0:rows, :], in0=xt[0:rows, :],
                                   scalar1=stats[0:rows, 2:3], scalar2=stats[0:rows, 4:5],
                                   op0=ALU.subtract, op1=ALU.mult)
        m_h = s["dve"].count

        if cidx < 0:
            for k in range(KD):
                pbank = pd[k % 2][:, 0:128]

                @ctx.op("tensor", waits=[(s["dve"], m_h), (s["act"], s["act"].count)],
                        inc=s["mm"])
                def _(e, hbf=hbf, k=k, pbank=pbank, c=c):
                    return e.transpose(pbank.bitcast(BF16)[:, 0:128], hbf[:, 128 * k:128 * (k + 1)],
                                       c["identb"][:])
                m_mm = s["mm"].count

                @ctx.op("scalar", waits=[(s["mm"], m_mm)])
                def _(e, h_fm=h_fm, k=k, pbank=pbank):
                    return e.activation(out=h_fm[:, TW * k + 13:TW * k + 16],
                                        in_=pbank.bitcast(BF16)[:, 0:3], func=AF.Copy)
            m_bnd = s["act"].count
        else:
            col0 = COL0 + CH * cidx

            @ctx.op("sync", waits=[(s["dve"], m_h)], inc=s["wtrh"], inc_n=16)
            def _(e, hbf=hbf, col0=col0, h_fm=h_fm):
                dst = h_fm.rearrange("p (k t) -> p k t", k=KD)
                return e.dma_start(out=dst[:, :, col0:col0 + 128],
                                   in_=hbf[:, :].rearrange("p (k t) -> p k t", k=KD),
                                   transpose=True)
            m_tr_prev = s["wtrh"].count

    ctx.m_h_fm = m_tr_prev
    ctx.m_h_act = s["act"].count
    ctx.h_barrier = [(s["dve"], s["dve"].count), (s["act"], s["act"].count),
                     (s["wtrh"], s["wtrh"].count), (s["wx"], s["wx"].count)]
    ctx.dbg_src["h_fm_out"] = (h_fm, [(s["wtrh"], ctx.m_h_fm), (s["act"], ctx.m_h_act)])
    A.release()



def plan_groups(ctx, phases):
    s, ap, A, c = ctx.s, ctx.ap, ctx.A, ctx.c
    pd = ctx.pdub
    h_fm = ctx.t["h_fm"]

    A.mark()
    wbuf = [A.bf(KD * 128), A.bf(KD * 128)]
    raw_hbc = A.bf(10 * TW)
    hbc_silu = A.bf(10 * 1024)
    raw_gate = A.bf(8 * 1024)
    raw_dt = A.f32(TW)
    conv_acc = A.bf(1024)
    cwg = A.f32(64)
    cbg = A.f32(16)
    normw_bc = A.f32(1024)
    dtT = A.f32(128)
    dtT_bf = A.bf(128)
    aneg = A.f32(128)
    expA = A.f32(128)
    decayT = A.bf(128)
    tbc = A.f32(128)
    nacumT = A.f32(128)
    lnsq = ctx.t["ln_sq"]
    hi_fm = lnsq[:, 0:512].bitcast(BF16)
    lo_fm = lnsq[:, 512:1024].bitcast(BF16)
    negfm = lnsq[:, 1024:2048]
    rowt = [lnsq[:, 2048:3072].bitcast(BF16), lnsq[:, 3072:4096].bitcast(BF16)]
    trA = A.bf(1280)
    x_h = A.bf(1024)
    xdec = A.bf(1024)
    s_core = A.f32(1024)
    prodt = A.f32(32)
    prev_bf = A.bf(1024)
    dm = raw_hbc[:, 0:2048].bitcast(F32)
    lt = raw_hbc[:, 2048:4096]
    lg = raw_hbc[:, 4096:6144]
    scr = [raw_hbc[:, 6144:8192].bitcast(F32),
           raw_hbc[:, 8192:10240].bitcast(F32),
           raw_dt[:, 0:1024]]
    g0m = A.bf(128)
    sin_run = [A.f32(1024), scr[0]]
    gg_bf = A.bf(1024)
    gg_st = A.bf(1024)
    trG = A.bf(1024)
    silu_g = A.bf(1024)
    coef_sb = aneg[0:8, 0:32]
    coef_bc = negfm[:, 896:1024]
    stats = A.f32(8)

    ctx.g = dict(wbuf=wbuf, raw_hbc=raw_hbc, hbc_silu=hbc_silu, raw_gate=raw_gate,
                 raw_dt=raw_dt, cwg=cwg, cbg=cbg, normw_bc=normw_bc, dtT=dtT,
                 dtT_bf=dtT_bf, aneg=aneg, expA=expA, decayT=decayT, tbc=tbc,
                 nacumT=nacumT, hi_fm=hi_fm, lo_fm=lo_fm, negfm=negfm, rowt=rowt,
                 conv_acc=conv_acc,
                 trA=trA, x_h=x_h, xdec=xdec, sin_run=sin_run, s_core=s_core,
                 prodt=prodt, prev_bf=prev_bf, dm=dm, lt=lt, lg=lg, g0m=g0m,
                 scr=scr, gg_bf=gg_bf, gg_st=gg_st, trG=trG, silu_g=silu_g,
                 coef_sb=coef_sb, coef_bc=coef_bc, stats=stats)

    @ctx.op("vector", waits=getattr(ctx, "h_barrier", []))
    def _(e, G=ctx.g):
        e.memset(G["raw_hbc"], 0.0)
        e.memset(G["raw_gate"], 0.0)
        e.memset(G["raw_dt"], 0.0)
        e.memset(G["hbc_silu"], 0.0)
        return e.memset(G["sin_run"][0], 0.0)
    m_z0 = s["dve"].count

    @ctx.op("sync", waits=[(s["dve"], m_z0)], inc=s["wacum"], inc_n=16)
    def _(e, G=ctx.g, ap=ctx.ap):
        return e.dma_start(out=ap["cc_in"][128:129, :], in_=G["sin_run"][0][0:1, :])
    ctx.m_ccfill = s["wacum"].count
    ctx.h_barrier = getattr(ctx, "h_barrier", []) + [(s["dve"], s["dve"].count)]
    ctx.m_group_done = 0
    for g in GROUPS:
        plan_group_loads(ctx, g)
        plan_inproj(ctx, g)
        plan_conv(ctx, g)
        if "SSD" in phases:
            plan_achain(ctx, g)
            if "P1" in SSD_PARTS:
                plan_pass1(ctx, g)
            if "CC" in SSD_PARTS:
                plan_gather(ctx, g)
            if "P2" in SSD_PARTS:
                plan_pass2(ctx, g)
        ctx.m_group_done = s["act"].count
        ctx.m_group_done_w = [(s["act"], s["act"].count), (s["dve"], s["dve"].count),
                              (s["mm"], s["mm"].count)]
    ctx.dbg_src["raw_hbc"] = (G["raw_hbc"] if False else ctx.g["raw_hbc"],
                              [(s["act"], s["act"].count), (s["dve"], s["dve"].count)])
    ctx.dbg_src["hbc_silu"] = (ctx.g["hbc_silu"],
                               [(s["act"], s["act"].count), (s["dve"], s["dve"].count)])
    ctx.dbg_src["raw_gate"] = (ctx.g["raw_gate"],
                               [(s["act"], s["act"].count), (s["dve"], s["dve"].count)])
    ctx.dbg_src["raw_dt"] = (ctx.g["raw_dt"],
                             [(s["act"], s["act"].count), (s["dve"], s["dve"].count)])
    allw = [(s["act"], s["act"].count), (s["dve"], s["dve"].count), (s["mm"], s["mm"].count)]
    ctx.dbg_src["dtT_o"] = (ctx.g["dtT"], allw)
    ctx.dbg_src["tbc_o"] = (ctx.g["tbc"], allw)
    ctx.dbg_src["nacumT_o"] = (ctx.g["nacumT"], allw)
    ctx.dbg_src["sc_o"] = (ctx.g["s_core"], allw)
    ctx.dbg_src["prev_o"] = (ctx.g["prev_bf"], allw)
    ctx.dbg_src["ys_o"] = (ctx.g["dm"], allw)
    ctx.dbg_src["xh_o"] = (ctx.g["x_h"], allw)
    ctx.dbg_src["lg_o"] = (ctx.g["lg"], allw)
    ctx.dbg_src["expA_o"] = (ctx.g["expA"], allw)
    ctx.dbg_src["hbc_o"] = (ctx.g["hbc_silu"], allw)
    ctx.dbg_src["sg_o"] = (ctx.g["silu_g"], allw)
    ctx.dbg_src["g_o"] = (ctx.g["scr"][1], allw)
    ctx.dbg_src["st_o"] = (ctx.g["stats"][:, 0:4], allw)
    ctx.dbg_src["gg_o"] = (ctx.g["gg_bf"], allw)
    ctx.dbg_src["trA_o"] = (ctx.g["trA"], allw)
    A.release()


def plan_group_loads(ctx, g):
    s, ap = ctx.s, ctx.ap
    G = ctx.g
    gate = (ctx.m_group_done_w if g != GROUPS[0] else []) + \
        getattr(ctx, "h_barrier", [])

    @ctx.op("gpsimd", waits=gate, inc=s["wsp"], inc_n=16)
    def _(e, G=G, g=g, ap=ap):
        return e.dma_start(out=G["cwg"][:, 0:40], in_=ap["conv_w_arr"][g])

    @ctx.op("gpsimd", waits=gate, inc=s["wsp"], inc_n=16)
    def _(e, G=G, g=g, ap=ap):
        return e.dma_start(out=G["cbg"][:, 0:10], in_=ap["conv_b_arr"][g])

    @ctx.op("gpsimd", waits=gate, inc=s["wsp"], inc_n=16)
    def _(e, G=G, g=g, ap=ap):
        return e.dma_start(out=G["normw_bc"][:, :],
                           in_=ap["norm_w"][0:1, 1024 * g:1024 * (g + 1)].partition_broadcast(128))
    ctx.m_gloads = s["wsp"].count


def plan_inproj(ctx, g):
    s, ap = ctx.s, ctx.ap
    G = ctx.g
    pd = ctx.pdub
    h_fm = ctx.t["h_fm"]
    h3 = h_fm.rearrange("p (k t) -> p k t", k=KD)
    ww = [s["ww0"], s["ww1"]]
    if not hasattr(ctx, "m_mmw"):
        ctx.m_evac = [0, 0]
        ctx.m_mmw = [0, 0]

    for j in range(NFT):
        par = j % 2
        wdma_waits = [(s["mm"], ctx.m_mmw[par])]
        if g == GROUPS[0] and j < 2:
            wdma_waits += getattr(ctx, "h_barrier", [])

        @ctx.op("gpsimd", waits=wdma_waits, inc=ww[par], inc_n=16)
        def _(e, G=G, g=g, j=j, par=par, ap=ap):
            return e.dma_start(out=G["wbuf"][par][:], in_=ap["w_in"][g, j])
        m_w = ww[par].count

        first_waits = [(ww[par], m_w), (s["act"], ctx.m_evac[par])]
        if j == 0:
            first_waits += [(s["wtrh"], ctx.m_h_fm), (s["act"], ctx.m_h_act)]
            first_waits += getattr(ctx, "h_barrier", [])

        @ctx.op("tensor", waits=first_waits, inc=s["mm"])
        def _(e, G=G, j=j, par=par, h3=h3, pd=pd):
            wv = G["wbuf"][par][:].rearrange("p (k f) -> p k f", k=KD)
            pA = pd[2 * par]
            pC = pd[2 * par + 1]
            last = None
            for k in range(KD):
                lhs = wv[:, k, :]
                e.matmul(pA[:, 0:512], lhs, h3[:, k, 13:525], start=(k == 0), stop=(k == KD - 1))
                e.matmul(pA[:, 512:1024], lhs, h3[:, k, 525:1037],
                         start=(k == 0), stop=(k == KD - 1))
                last = e.matmul(pC[:, 0:3], lhs, h3[:, k, 1037:1040],
                                start=(k == 0), stop=(k == KD - 1))
            return last
        ctx.m_mmw[par] = s["mm"].count
        m_mm = s["mm"].count

        if j < 10 or j == 18:
            dst = G["raw_dt"] if j == 18 else G["raw_hbc"][:, TW * j:TW * (j + 1)]

            @ctx.op("scalar", waits=[(s["mm"], m_mm)])
            def _(e, dst=dst, par=par, pd=pd):
                pA = pd[2 * par]
                pC = pd[2 * par + 1]
                e.activation(out=dst[:, 13:525], in_=pA[:, 0:512], func=AF.Copy)
                e.activation(out=dst[:, 525:1037], in_=pA[:, 512:1024], func=AF.Copy)
                return e.activation(out=dst[:, 1037:1040], in_=pC[:, 0:3], func=AF.Copy)
        else:
            jj = j - 10
            gt = G["raw_gate"]

            def gview(t0, t1, jj=jj, gt=gt):
                c0 = t0 // 128
                if t1 - t0 > 128:
                    assert t0 % 128 == 0 and t1 % 128 == 0
                    return bass.AP(gt.tensor, gt.offset + 1024 * c0 + 128 * jj,
                                   [list(gt.ap[0]), [1024, (t1 - t0) // 128], [1, 128]])
                return bass.AP(gt.tensor, gt.offset + 1024 * c0 + 128 * jj + (t0 % 128),
                               [list(gt.ap[0]), [1, t1 - t0]])

            @ctx.op("scalar", waits=[(s["mm"], m_mm)])
            def _(e, gview=gview, par=par, pd=pd):
                pA = pd[2 * par][:, 0:512]
                pB = pd[2 * par][:, 512:1024]
                pC = pd[2 * par + 1][:, 0:3]
                last = None
                for t0, t1, seg, c0 in [(0, 384, pA, 3), (384, 509, pA, 387),
                                        (509, 512, pB, 0), (512, 896, pB, 3),
                                        (896, 1021, pB, 387), (1021, 1024, pC, 0)]:
                    last = e.activation(out=gview(t0, t1), in_=seg[:, c0:c0 + (t1 - t0)],
                                        func=AF.Copy)
                return last
        ctx.m_evac[par] = s["act"].count
    ctx.m_inproj_mm = s["mm"].count
    ctx.m_inproj_evac = s["act"].count


def plan_conv(ctx, g):
    s = ctx.s
    G = ctx.g
    for j in range(10):
        raw = G["raw_hbc"][:, TW * j:TW * (j + 1)]
        out = bass.AP(G["hbc_silu"].tensor, G["hbc_silu"].offset + 128 * j,
                      [list(G["hbc_silu"].ap[0]), [1280, 8], [1, 128]])
        wcols = [G["cwg"][:, 4 * j + t:4 * j + t + 1] for t in range(4)]
        bcol = G["cbg"][:, j:j + 1]
        w0 = [(s["act"], s["act"].count), (s["wsp"], ctx.m_gloads)]

        @ctx.op("vector", waits=w0)
        def _(e, raw=raw, wcols=wcols, G=G):
            return e.tensor_scalar(out=G["conv_acc"],
                                   in0=raw[:, COL0 - 3:COL0 - 3 + 1024],
                                   scalar1=wcols[0], scalar2=None, op0=ALU.mult)
        m0 = s["dve"].count

        for t in range(1, 4):
            @ctx.op("vector", waits=[(s["dve"], s["dve"].count)])
            def _(e, raw=raw, wcols=wcols, t=t, G=G):
                return e.scalar_tensor_tensor(out=G["conv_acc"],
                                              in0=raw[:, COL0 - 3 + t:COL0 - 3 + t + 1024],
                                              scalar=wcols[t],
                                              in1=G["conv_acc"],
                                              op0=ALU.mult, op1=ALU.add)
        m_conv = s["dve"].count

        @ctx.op("scalar", waits=[(s["dve"], m_conv)])
        def _(e, out=out, bcol=bcol, G=G):
            return e.activation(out=out, in_=G["conv_acc"], func=AF.Silu, bias=bcol)
    ctx.m_conv = s["act"].count


def plan_achain(ctx, g):
    s, ap, c = ctx.s, ctx.ap, ctx.c
    G = ctx.g
    pd = ctx.pdub
    lnsq = ctx.t["ln_sq"]
    nb = [G["negfm"][:, 128 * i:128 * (i + 1)] for i in range(8)]
    dtT_raw = nb[0]

    for cx in range(NCH):
        pb = pd[2 + cx % 2]
        @ctx.op("tensor", waits=[(s["act"], ctx.m_inproj_evac),
                                 (s["act"], ctx.m_achain_ev if False else 0),
                                 (s["act"], s["act"].count)], inc=s["mm"])
        def _(e, G=G, cx=cx, pb=pb, c=c):
            return e.transpose(pb[:, 0:128], G["raw_dt"][:, COL0 + 128 * cx:COL0 + 128 * (cx + 1)],
                               c["identf"][:])
        m_mm = s["mm"].count

        @ctx.op("scalar", waits=[(s["mm"], m_mm)])
        def _(e, G=G, cx=cx, pb=pb, dtT_raw=dtT_raw):
            return e.activation(out=dtT_raw[:, 16 * cx:16 * (cx + 1)], in_=pb[:, 0:16],
                                func=AF.Copy)
    m_dtT = s["act"].count

    dtbv = c["dtbbc"][:, HPG * g:HPG * (g + 1)].unsqueeze(1).broadcast_to([128, 8, HPG])

    @ctx.op("vector", waits=[(s["act"], m_dtT)])
    def _(e, nb=nb, dtbv=dtbv):
        return e.tensor_tensor(out=nb[1].rearrange("p (c h) -> p c h", c=8),
                               in0=nb[0].rearrange("p (c h) -> p c h", c=8),
                               in1=dtbv, op=ALU.add)
    m_z = s["dve"].count

    @ctx.op("scalar", waits=[(s["dve"], m_z)])
    def _(e, nb=nb):
        return e.activation(out=nb[2], in_=nb[1], func=AF.Abs)
    m1 = s["act"].count

    @ctx.op("scalar", waits=[(s["act"], m1)])
    def _(e, nb=nb):
        return e.activation(out=nb[3], in_=nb[2], func=AF.Exp, scale=-1.0)
    m2 = s["act"].count

    @ctx.op("scalar", waits=[(s["act"], m2)])
    def _(e, nb=nb, c=ctx.c):
        return e.activation(out=nb[2], in_=nb[3], func=AF.Ln, bias=c["onescol"][:, 0:1])
    m3 = s["act"].count

    @ctx.op("scalar", waits=[(s["act"], m3)])
    def _(e, nb=nb):
        return e.activation(out=nb[3], in_=nb[1], func=AF.Relu)
    m4 = s["act"].count

    @ctx.op("vector", waits=[(s["act"], m4)])
    def _(e, nb=nb):
        return e.tensor_add(nb[1], nb[2], nb[3])
    m5 = s["dve"].count

    @ctx.op("vector", waits=[(s["dve"], m5)])
    def _(e, nb=nb, G=G):
        return e.tensor_scalar(out=G["dtT"], in0=nb[1], scalar1=DT_MAX, scalar2=None,
                               op0=ALU.min)
    m_dt = s["dve"].count

    @ctx.op("vector", waits=[(s["dve"], m_dt)])
    def _(e, G=G):
        return e.tensor_copy(out=G["dtT_bf"], in_=G["dtT"])
    ctx.m_dtbf = s["dve"].count

    eav = c["expalog"][:, HPG * g:HPG * (g + 1)].unsqueeze(1).broadcast_to([128, 8, HPG])

    @ctx.op("vector", waits=[(s["dve"], m_dt)])
    def _(e, G=G, eav=eav):
        return e.tensor_tensor(out=G["aneg"].rearrange("p (c h) -> p c h", c=8),
                               in0=G["dtT"].rearrange("p (c h) -> p c h", c=8),
                               in1=eav, op=ALU.mult)
    m_aneg = s["dve"].count

    @ctx.op("tensor", waits=[(s["dve"], m_aneg), (s["act"], s["act"].count)], inc=s["mm"])
    def _(e, G=G, c=c, pd=pd):
        last = None
        for cx in range(NCH):
            rhs = G["aneg"][:, 16 * cx:16 * (cx + 1)]
            e.matmul(pd[0][:, 16 * cx:16 * (cx + 1)], c["uincl"][:], rhs, start=True, stop=True)
            e.matmul(pd[0][:, 128 + 16 * cx:128 + 16 * (cx + 1)], c["ustrict"][:], rhs,
                     start=True, stop=True)
            e.matmul(pd[1][0:16, 128 * cx:128 * (cx + 1)], rhs, c["uincl"][:],
                     start=True, stop=True)
            last = e.matmul(pd[0][0:1, 512 + 16 * cx:512 + 16 * (cx + 1)],
                            c["onescol"][:], rhs, start=True, stop=True)
        return last
    m_cs = s["mm"].count

    @ctx.op("scalar", waits=[(s["mm"], m_cs)])
    def _(e, G=G, pd=pd):
        return e.activation(out=G["nacumT"], in_=pd[0][:, 0:128], func=AF.Copy)
    m_nacum = s["act"].count

    @ctx.op("scalar", waits=[(s["mm"], m_cs)])
    def _(e, G=G, pd=pd):
        return e.activation(out=G["expA"], in_=pd[0][:, 0:128], func=AF.Exp, scale=-1.0)

    @ctx.op("scalar", waits=[(s["mm"], m_cs)])
    def _(e, G=G, pd=pd):
        return e.activation(out=G["decayT"], in_=pd[0][:, 128:256], func=AF.Exp, scale=-1.0)

    @ctx.op("scalar", waits=[(s["mm"], m_cs)])
    def _(e, G=G, pd=pd):
        return e.activation(out=G["aneg"][0:1, 0:128], in_=pd[0][0:1, 512:640],
                            func=AF.Copy)
    m_ev1 = s["act"].count

    @ctx.op("vector", waits=[(s["act"], m_ev1)])
    def _(e, G=G):
        v = G["aneg"][0:1, 0:128].rearrange("p (c h) -> p h c", c=8)
        return e.reduce_sum(out=G["aneg"][96:97, 0:16], in_=v, axis=AX.X)
    m_nacore = s["dve"].count

    @ctx.op("tensor", waits=[(s["act"], m_ev1)], inc=s["mm"])
    def _(e, G=G, c=c, pd=pd):
        return e.matmul(pd[2][:, 0:128], c["onesrow_f"][0:1, :], G["aneg"][0:1, 0:128],
                        start=True, stop=True)
    m_tb = s["mm"].count

    @ctx.op("scalar", waits=[(s["mm"], m_tb)])
    def _(e, G=G, pd=pd):
        return e.activation(out=G["tbc"], in_=pd[2][:, 0:128], func=AF.Exp, scale=-1.0)
    ctx.m_tbc = s["act"].count

    @ctx.op("vector", waits=[(s["mm"], m_cs)])
    def _(e, G=G, pd=pd):
        return e.tensor_scalar(out=G["negfm"][0:16, 0:1024], in0=pd[1][0:16, 0:1024],
                               scalar1=-1.0, scalar2=None, op0=ALU.mult)
    m_n1 = s["dve"].count

    @ctx.op("vector", waits=[(s["dve"], m_n1)])
    def _(e, G=G):
        return e.tensor_copy(out=G["hi_fm"][0:16, 0:1024], in_=G["negfm"][0:16, 0:1024])
    m_n2 = s["dve"].count

    @ctx.op("vector", waits=[(s["dve"], m_n2)])
    def _(e, G=G):
        return e.tensor_copy(out=G["scr"][1][0:16, 0:1024], in_=G["hi_fm"][0:16, 0:1024])
    m_n3 = s["dve"].count

    @ctx.op("vector", waits=[(s["dve"], m_n3)])
    def _(e, G=G):
        return e.tensor_tensor(out=G["lo_fm"][0:16, 0:1024], in0=G["negfm"][0:16, 0:1024],
                               in1=G["scr"][1][0:16, 0:1024], op=ALU.subtract)
    m_n4 = s["dve"].count

    @ctx.op("sync", waits=[(s["dve"], m_n4)], inc=s["wacum"], inc_n=16)
    def _(e, G=G, ap=ctx.ap):
        return e.dma_start(out=ap["acum_hilo"][0], in_=G["hi_fm"][0:16, 0:1024])

    @ctx.op("sync", waits=[(s["dve"], m_n4)], inc=s["wacum"], inc_n=16)
    def _(e, G=G, ap=ctx.ap):
        return e.dma_start(out=ap["acum_hilo"][1], in_=G["lo_fm"][0:16, 0:1024])
    ctx.m_hilo = s["wacum"].count
    ctx.m_nacore = m_nacore


def plan_pass1(ctx, g):
    s, ap = ctx.s, ctx.ap
    G = ctx.g
    pd = ctx.pdub
    wtr = [s["wtra"], s["wtrb"]]
    wsp = [s["wspa"], s["wspb"]]
    m_states = 0
    ctx.m_p1_dve = s["dve"].count

    for cx in range(NCH):
        par = cx % 2
        @ctx.op("sync", waits=[(s["act"], ctx.m_conv), (s["mm"], m_states),
                               (s["dve"], ctx.m_p1_dve), (s["wsl"], s["wsl"].count)],
                inc=wtr[par], inc_n=16)
        def _(e, G=G, cx=cx):
            inview = G["hbc_silu"][:, 1280 * cx:1280 * (cx + 1)].rearrange(
                "p (j t) -> p j t", j=10)
            return e.dma_start(out=G["trA"].rearrange("p (j t) -> p j t", j=10),
                               in_=inview, transpose=True)
        m_tr = wtr[par].count

        @ctx.op("sync", waits=[(wtr[par], m_tr)], inc=s["wsl"], inc_n=16)
        def _(e, G=G, cx=cx, ap=ap):
            return e.dma_start(out=ap["hstm_d"][cx], in_=G["trA"][:, 0:1024])

        dtv = G["dtT_bf"][:, 16 * cx:16 * (cx + 1)].unsqueeze(2).broadcast_to([128, HPG, P])

        @ctx.op("vector", waits=[(wtr[par], m_tr), (s["dve"], ctx.m_dtbf)])
        def _(e, G=G, dtv=dtv):
            return e.tensor_tensor(out=G["x_h"].rearrange("p (h q) -> p h q", h=HPG),
                                   in0=G["trA"][:, 0:1024].rearrange("p (h q) -> p h q", h=HPG),
                                   in1=dtv, op=ALU.mult)
        m_xh = s["dve"].count

        dcv = G["decayT"][:, 16 * cx:16 * (cx + 1)].unsqueeze(2).broadcast_to([128, HPG, P])

        @ctx.op("vector", waits=[(s["dve"], m_xh), (s["act"], ctx.m_tbc)])
        def _(e, G=G, dcv=dcv):
            return e.tensor_tensor(out=G["xdec"].rearrange("p (h q) -> p h q", h=HPG),
                                   in0=G["x_h"].rearrange("p (h q) -> p h q", h=HPG),
                                   in1=dcv, op=ALU.mult)
        m_xd = s["dve"].count

        @ctx.op("tensor", waits=[(s["dve"], m_xd)], inc=s["mm"])
        def _(e, G=G, pd=pd):
            last = None
            for h in range(HPG):
                last = e.matmul(pd[0][:, P * h:P * (h + 1)], G["trA"][:, 1024:1152],
                                G["xdec"][:, P * h:P * (h + 1)], start=True, stop=True)
            return last
        m_states = s["mm"].count

        sold = G["sin_run"][cx % 2]
        snew = G["sin_run"][(cx + 1) % 2]
        tbv = G["tbc"][:, 16 * cx:16 * (cx + 1)].unsqueeze(2).broadcast_to([128, HPG, P])

        if cx > 0:
            @ctx.op("sync", waits=[(s["dve"], s["dve"].count)], inc=s["wsl"], inc_n=16)
            def _(e, sold=sold, cx=cx, ap=ap):
                return e.dma_start(out=ap["sinloc_d"][cx], in_=sold)

        if cx == 0:
            @ctx.op("vector", waits=[(s["mm"], m_states), (s["wsl"], s["wsl"].count),
                                     (s["wacum"], s["wacum"].count)])
            def _(e, snew=snew, pd=pd):
                return e.tensor_copy(out=snew, in_=pd[0][:, 0:1024])
        else:
            @ctx.op("vector", waits=[(s["mm"], m_states), (s["act"], ctx.m_tbc),
                                     (s["wsl"], s["wsl"].count), (s["wacum"], s["wacum"].count)])
            def _(e, G=G, sold=sold, tbv=tbv):
                return e.tensor_tensor(out=G["scr"][2].rearrange("p (h q) -> p h q", h=HPG),
                                       in0=sold.rearrange("p (h q) -> p h q", h=HPG),
                                       in1=tbv, op=ALU.mult)
            m_t = s["dve"].count

            @ctx.op("vector", waits=[(s["dve"], m_t), (s["wsl"], s["wsl"].count)])
            def _(e, G=G, snew=snew, pd=pd):
                return e.tensor_add(snew, G["scr"][2], pd[0][:, 0:1024])
        ctx.m_p1_dve = s["dve"].count

    ctx.m_score = ctx.m_p1_dve
    ctx.m_p1_states = m_states


def plan_gather(ctx, g):
    s, ap = ctx.s, ctx.ap
    G = ctx.g

    @ctx.op("sync", waits=[(s["dve"], ctx.m_score)], inc=s["wacum"], inc_n=16)
    def _(e, G=G, ap=ap):
        return e.dma_start(out=ap["cc_in"][0:128, :], in_=G["sin_run"][0])

    @ctx.op("sync", waits=[(s["dve"], ctx.m_nacore)], inc=s["wacum"], inc_n=16)
    def _(e, G=G, ap=ap):
        return e.dma_start(out=ap["cc_in"][128:129, 0:HPG], in_=G["aneg"][96:97, 0:16])
    m_ccin = s["wacum"].count

    @ctx.op("gpsimd", waits=[(s["wacum"], m_ccin), (s["cc"], s["cc"].count)],
            inc=s["cc"], inc_n=1)
    def _(e, ap=ap):
        return e.collective_compute(
            "AllGather", ALU.bypass, replica_groups=[list(range(NC))],
            ins=[ap["cc_in"]], outs=[ap["cc_out"]])
    ctx.m_cc = s["cc"].count


def plan_pass2(ctx, g):
    s, ap, c = ctx.s, ctx.ap, ctx.c
    G = ctx.g
    pd = ctx.pdub
    lnq = ctx.t["ln_sq"]

    @ctx.op("sync", waits=[(s["cc"], ctx.m_cc)], inc=s["wrda"], inc_n=16)
    def _(e, G=G, ap=ap):
        src = ap["cc_out"].rearrange("(j r) n -> j r n", r=129)[:, 128, 0:HPG]
        return e.dma_start(out=G["coef_sb"][0:8, 0:16], in_=src)
    m_ac = s["wrda"].count

    @ctx.op("tensor", waits=[(s["wrda"], m_ac), (s["mm"], s["mm"].count)], inc=s["mm"])
    def _(e, G=G, c=c, pd=pd):
        return e.matmul(pd[2][0:8, 128:144], c["mi"][0:8, 0:8], G["coef_sb"][0:8, 0:16],
                        start=True, stop=True)
    m_cp = s["mm"].count

    @ctx.op("scalar", waits=[(s["mm"], m_cp)])
    def _(e, G=G, c=c, pd=pd):
        return e.activation(out=G["coef_sb"][0:8, 16:32], in_=pd[2][0:8, 128:144],
                            func=AF.Exp, scale=-1.0, bias=c["maskcol"][0:8, 0:1])
    m_coef = s["act"].count

    @ctx.op("sync", waits=[(s["act"], m_coef)], inc=s["wacum"], inc_n=16)
    def _(e, G=G, ap=ap):
        return e.dma_start(out=ap["coef_b"][0:1, :], in_=G["coef_sb"][0:8, 16:32])
    m_cb = s["wacum"].count

    @ctx.op("sync", waits=[(s["wacum"], m_cb)], inc=s["wacum"], inc_n=16)
    def _(e, G=G, ap=ap):
        return e.dma_start(out=G["aneg"][32:33, 0:128], in_=ap["coef_b"][0:1, :])
    m_cb2 = s["wacum"].count

    @ctx.op("tensor", waits=[(s["wacum"], m_cb2)], inc=s["mm"])
    def _(e, G=G, c=c, pd=pd):
        return e.matmul(pd[2][:, 256:384], c["onesrow_f"][32:33, :], G["aneg"][32:33, 0:128],
                        start=True, stop=True)
    m_cbc = s["mm"].count

    @ctx.op("scalar", waits=[(s["mm"], m_cbc)])
    def _(e, G=G, pd=pd):
        return e.activation(out=G["coef_bc"], in_=pd[2][:, 256:384], func=AF.Copy)
    m_cbc2 = s["act"].count

    wrd = [s["wrda"], s["wrdb"]]
    m_sc_dve = s["dve"].count
    for j in range(NC):
        par = j % 2

        @ctx.op("sync", waits=[(s["cc"], ctx.m_cc), (s["dve"], m_sc_dve)],
                inc=wrd[par], inc_n=16)
        def _(e, G=G, j=j, ap=ap):
            return e.dma_start(out=G["scr"][1], in_=ap["cc_out"][129 * j:129 * j + 128, :])
        m_ld = wrd[par].count

        cview = G["coef_bc"][:, 16 * j:16 * (j + 1)].unsqueeze(2).broadcast_to([128, HPG, P])

        @ctx.op("vector", waits=[(wrd[par], m_ld), (s["act"], m_cbc2)])
        def _(e, G=G, cview=cview, j=j):
            if j == 0:
                return e.tensor_tensor(out=G["s_core"].rearrange("p (h q) -> p h q", h=HPG),
                                       in0=G["scr"][1].rearrange("p (h q) -> p h q", h=HPG),
                                       in1=cview, op=ALU.mult)
            return e.tensor_tensor(out=G["scr"][2].rearrange("p (h q) -> p h q", h=HPG),
                                   in0=G["scr"][1].rearrange("p (h q) -> p h q", h=HPG),
                                   in1=cview, op=ALU.mult)
        m_m = s["dve"].count

        if j > 0:
            @ctx.op("vector", waits=[(s["dve"], m_m)])
            def _(e, G=G):
                return e.tensor_add(G["s_core"], G["s_core"], G["scr"][2])
        m_sc_dve = s["dve"].count
    m_score_core = s["dve"].count

    if "P2C" in SSD_PARTS:
        return
    @ctx.op("vector", waits=[(s["dve"], m_score_core)])
    def _(e, G=G):
        return e.memset(G["prodt"][:, 0:16], 1.0)
    m_pt = [s["dve"].count]

    wr2 = [s["wr2a"], s["wr2b"]]
    wrh = [s["wrha"], s["wrhb"]]
    wgg = [s["wgga"], s["wggb"]]
    sqt = G["raw_hbc"][:, 2048:4096].bitcast(F32)
    m_y = s["mm"].count
    m_chunk_dve = s["dve"].count
    m_chunk_act = s["act"].count
    for cx in range(NCH):
        par = cx % 2
        @ctx.op("sync", waits=[(s["wacum"], ctx.m_hilo), (s["mm"], m_y)],
                inc=wr2[par], inc_n=16)
        def _(e, G=G, cx=cx, par=par, ap=ap):
            hi = ap["acum_hilo"][0][:, 128 * cx:128 * (cx + 1)].unsqueeze(0)
            return e.dma_start(out=G["rowt"][0][0:1, 0:2048].rearrange("p (h t) -> p h t", h=16),
                               in_=hi)

        @ctx.op("sync", waits=[(s["wacum"], ctx.m_hilo), (s["mm"], m_y)],
                inc=wr2[par], inc_n=16)
        def _(e, G=G, cx=cx, par=par, ap=ap):
            lo = ap["acum_hilo"][1][:, 128 * cx:128 * (cx + 1)].unsqueeze(0)
            return e.dma_start(out=G["rowt"][1][0:1, 0:2048].rearrange("p (h t) -> p h t", h=16),
                               in_=lo)
        m_row = wr2[par].count

        @ctx.op("sync", waits=[(s["wsl"], s["wsl"].count), (s["dve"], m_chunk_dve)],
                inc=wrh[par], inc_n=16)
        def _(e, G=G, cx=cx, ap=ap):
            return e.dma_start(out=G["trA"][:, 0:1024], in_=ap["hstm_d"][cx])

        if cx > 0:
            @ctx.op("sync", waits=[(s["dve"], m_chunk_dve)], inc=wrh[par], inc_n=16)
            def _(e, G=G, cx=cx, ap=ap):
                return e.dma_start(out=G["scr"][1], in_=ap["sinloc_d"][cx])

        @ctx.op("sync", waits=[(s["act"], m_chunk_act), (s["dve"], m_chunk_dve)],
                inc=wrh[par], inc_n=16)
        def _(e, G=G, cx=cx):
            return e.dma_start(out=G["trG"].rearrange("p (j t) -> p j t", j=8),
                               in_=G["raw_gate"][:, 1024 * cx:1024 * (cx + 1)].rearrange(
                                   "p (j t) -> p j t", j=8),
                               transpose=True)
        m_hs = m_sl = m_tg = wrh[par].count

        @ctx.op("scalar", waits=[(wrh[par], m_tg)])
        def _(e, G=G):
            return e.activation(out=G["silu_g"], in_=G["trG"], func=AF.Silu)
        m_sg = s["act"].count

        dtv = G["dtT_bf"][:, 16 * cx:16 * (cx + 1)].unsqueeze(2).broadcast_to([128, HPG, P])

        @ctx.op("vector", waits=[(wrh[par], m_hs)])
        def _(e, G=G, dtv=dtv):
            return e.tensor_tensor(out=G["x_h"].rearrange("p (h q) -> p h q", h=HPG),
                                   in0=G["trA"][:, 0:1024].rearrange("p (h q) -> p h q", h=HPG),
                                   in1=dtv, op=ALU.mult)
        m_xh = s["dve"].count

        @ctx.op("tensor", waits=[(s["act"], ctx.m_conv), (s["dve"], m_chunk_dve)], inc=s["mm"])
        def _(e, G=G, cx=cx, pd=pd):
            B = G["hbc_silu"][:, 1280 * cx + 1024:1280 * cx + 1152]
            C = G["hbc_silu"][:, 1280 * cx + 1152:1280 * cx + 1280]
            return e.matmul(pd[2][:, 0:128], B, C, start=True, stop=True)
        m_g0 = s["mm"].count

        @ctx.op("vector", waits=[(s["mm"], m_g0)])
        def _(e, G=G, c=c, pd=pd):
            return e.tensor_tensor(out=G["g0m"], in0=pd[2][:, 0:128], in1=c["g0mask"][:],
                                   op=ALU.mult)
        m_g0m = s["dve"].count

        if "P2M" in SSD_PARTS:
            m_chunk_dve = s["dve"].count
            m_chunk_act = s["act"].count
            continue
        for half in range(2):
            @ctx.op("tensor", waits=[(wr2[par], m_row), (s["dve"], s["dve"].count),
                                     (s["act"], s["act"].count)], inc=s["mm"])
            def _(e, G=G, c=c, pd=pd, par=par, half=half):
                last = None
                for blk in range(2):
                    seg = slice(512 * blk, 512 * (blk + 1))
                    off = 1024 * half + 512 * blk
                    e.matmul(pd[3][:, seg], c["onesrow"][0:1, :],
                             G["rowt"][0][0:1, off:off + 512], start=True, stop=False)
                    e.matmul(pd[3][:, seg], c["onesrow"][0:1, :],
                             G["rowt"][1][0:1, off:off + 512], start=False, stop=False)
                    last = e.matmul(pd[3][:, seg], c["identb"][:], c["maskblk"][:, 0:512],
                                    start=False, stop=True)
                return last
            m_dmm = s["mm"].count

            nv = G["nacumT"][:, 16 * cx + 8 * half:16 * cx + 8 * (half + 1)] \
                .unsqueeze(2).broadcast_to([128, 8, 128])

            @ctx.op("vector", waits=[(s["mm"], m_dmm)])
            def _(e, G=G, nv=nv, pd=pd):
                return e.tensor_tensor(out=G["dm"].rearrange("p (h l) -> p h l", h=8),
                                       in0=pd[3][:, 0:1024].rearrange("p (h l) -> p h l", h=8),
                                       in1=nv, op=ALU.add)
            m_dm = s["dve"].count

            @ctx.op("scalar", waits=[(s["dve"], m_dm)])
            def _(e, G=G, half=half):
                return e.activation(out=G["lt"][:, 1024 * half:1024 * (half + 1)],
                                    in_=G["dm"], func=AF.Exp)
        m_lt = s["act"].count

        if "P2L" in SSD_PARTS:
            m_chunk_dve = s["dve"].count
            m_chunk_act = s["act"].count
            continue
        gv = G["g0m"].unsqueeze(1).broadcast_to([128, HPG, 128])

        @ctx.op("vector", waits=[(s["act"], m_lt), (s["dve"], m_g0m)])
        def _(e, G=G, gv=gv):
            return e.tensor_tensor(out=G["lg"].rearrange("p (h l) -> p h l", h=HPG),
                                   in0=G["lt"].rearrange("p (h l) -> p h l", h=HPG),
                                   in1=gv, op=ALU.mult)
        m_lg = s["dve"].count

        ptv = G["prodt"][:, 16 * (cx % 2):16 * (cx % 2) + 16] \
            .unsqueeze(2).broadcast_to([128, HPG, P])

        @ctx.op("vector", waits=[(s["dve"], max(m_score_core, m_pt[0])), (wrh[par], m_sl)])
        def _(e, G=G, ptv=ptv):
            return e.tensor_tensor(out=G["scr"][2].rearrange("p (h q) -> p h q", h=HPG),
                                   in0=G["s_core"].rearrange("p (h q) -> p h q", h=HPG),
                                   in1=ptv, op=ALU.mult)
        m_pv = s["dve"].count

        @ctx.op("vector", waits=[(s["dve"], m_pv)])
        def _(e, G=G, cx=cx):
            if cx == 0:
                return e.tensor_copy(out=G["prev_bf"], in_=G["scr"][2])
            return e.tensor_tensor(out=G["prev_bf"], in0=G["scr"][2], in1=G["scr"][1],
                                   op=ALU.add)
        m_prev = s["dve"].count

        tbv = G["tbc"][:, 16 * cx:16 * (cx + 1)]

        @ctx.op("vector", waits=[(s["dve"], m_prev)])
        def _(e, G=G, cx=cx, tbv=tbv):
            return e.tensor_tensor(out=G["prodt"][:, 16 * ((cx + 1) % 2):16 * ((cx + 1) % 2) + 16],
                                   in0=G["prodt"][:, 16 * (cx % 2):16 * (cx % 2) + 16],
                                   in1=tbv, op=ALU.mult)
        m_pt[0] = s["dve"].count

        @ctx.op("tensor", waits=[(s["dve"], max(m_lg, m_xh, m_prev))], inc=s["mm"])
        def _(e, G=G, pd=pd, cx=cx):
            Ct = G["hbc_silu"][:, 1280 * cx + 1152:1280 * cx + 1280]
            last = None
            for h in range(HPG):
                e.matmul(pd[0][:, P * h:P * (h + 1)], G["lg"][:, 128 * h:128 * (h + 1)],
                         G["x_h"][:, P * h:P * (h + 1)], start=True, stop=True)
            for h in range(HPG):
                last = e.matmul(pd[1][:, P * h:P * (h + 1)], Ct,
                                G["prev_bf"][:, P * h:P * (h + 1)], start=True, stop=True)
            return last
        m_y = s["mm"].count

        eav = G["expA"][:, 16 * cx:16 * (cx + 1)].unsqueeze(2).broadcast_to([128, HPG, P])

        @ctx.op("vector", waits=[(s["mm"], m_y)])
        def _(e, G=G, eav=eav, pd=pd):
            return e.tensor_tensor(out=G["scr"][2].rearrange("p (h q) -> p h q", h=HPG),
                                   in0=pd[1][:, 0:1024].rearrange("p (h q) -> p h q", h=HPG),
                                   in1=eav, op=ALU.mult)
        m_e1 = s["dve"].count

        @ctx.op("vector", waits=[(s["dve"], m_e1)])
        def _(e, G=G, pd=pd):
            return e.tensor_add(G["scr"][0], G["scr"][2], pd[0][:, 0:1024])
        m_e2 = s["dve"].count

        dbv = c["dbc"][:, HPG * g:HPG * (g + 1)].unsqueeze(2).broadcast_to([128, HPG, P])

        @ctx.op("vector", waits=[(s["dve"], m_e2)])
        def _(e, G=G, dbv=dbv):
            return e.tensor_tensor(out=G["scr"][2].rearrange("p (h q) -> p h q", h=HPG),
                                   in0=G["trA"][:, 0:1024].rearrange("p (h q) -> p h q", h=HPG),
                                   in1=dbv, op=ALU.mult)
        m_e3 = s["dve"].count

        @ctx.op("vector", waits=[(s["dve"], m_e3)])
        def _(e, G=G):
            return e.tensor_add(G["dm"], G["scr"][0], G["scr"][2])
        m_e4 = s["dve"].count

        @ctx.op("vector", waits=[(s["dve"], m_e4), (s["act"], m_sg)])
        def _(e, G=G):
            return e.tensor_mul(G["scr"][1], G["dm"], G["silu_g"])
        m_e5 = s["dve"].count

        @ctx.op("scalar", waits=[(s["dve"], m_e5)])
        def _(e, G=G, sqt=sqt):
            return e.activation(out=sqt, in_=G["scr"][1], func=AF.Square)
        m_e6 = s["act"].count

        @ctx.op("vector", waits=[(s["act"], m_e6)])
        def _(e, G=G, sqt=sqt):
            return e.reduce_sum(out=G["stats"][:, 0:1], in_=sqt, axis=AX.X)
        m_e7 = s["dve"].count

        @ctx.op("vector", waits=[(s["dve"], m_e7)])
        def _(e, G=G):
            return e.tensor_scalar(out=G["stats"][:, 1:2], in0=G["stats"][:, 0:1],
                                   scalar1=1.0 / 1024, scalar2=None, op0=ALU.mult)
        m_e8 = s["dve"].count

        @ctx.op("scalar", waits=[(s["dve"], m_e8)])
        def _(e, G=G, c=c):
            return e.activation(out=G["stats"][:, 2:3], in_=G["stats"][:, 1:2], func=AF.Ln,
                                bias=ctx.t["eps_col"][:, 0:1])
        m_e9 = s["act"].count

        @ctx.op("scalar", waits=[(s["act"], m_e9)])
        def _(e, G=G):
            return e.activation(out=G["stats"][:, 3:4], in_=G["stats"][:, 2:3], func=AF.Exp,
                                scale=-0.5)
        m_e10 = s["act"].count

        @ctx.op("vector", waits=[(s["act"], m_e10), (s["wsp"], ctx.m_gloads)])
        def _(e, G=G):
            return e.scalar_tensor_tensor(out=G["gg_bf"], in0=G["scr"][1],
                                          scalar=G["stats"][:, 3:4], in1=G["normw_bc"],
                                          op0=ALU.mult, op1=ALU.mult)
        m_gg = s["dve"].count

        @ctx.op("sync", waits=[(s["dve"], m_gg), (wgg[1 - par], wgg[1 - par].count)],
                inc=wgg[par], inc_n=16)
        def _(e, G=G):
            return e.dma_start(out=G["gg_st"].rearrange("p (j t) -> p j t", j=8),
                               in_=G["gg_bf"].rearrange("p (j t) -> p j t", j=8),
                               transpose=True)
        m_gt = wgg[par].count

        @ctx.op("sync", waits=[(wgg[par], m_gt)], inc=wgg[par], inc_n=16)
        def _(e, G=G, cx=cx, g=g, ap=ap):
            return e.dma_start(out=ap["gg_fm_d"][8 * g:8 * (g + 1), cx].rearrange("k p t -> p k t"),
                               in_=G["gg_st"].rearrange("p (j t) -> p j t", j=8))
        m_chunk_dve = s["dve"].count
        m_chunk_act = s["act"].count
    ctx.m_p2_end = dict(dve=s["dve"].count, act=s["act"].count, mm=s["mm"].count,
                        wgg=(wgg[0].count, wgg[1].count))


def plan_outproj(ctx):
    s, ap, A, c = ctx.s, ctx.ap, ctx.A, ctx.c
    pd = ctx.pdub
    mix_sb = A.bf(NCH * D)
    x2_sb = A.bf(NCH * D)
    xt2 = A.f32(D)
    stats2 = A.f32(16)
    ffst = A.f32(NCH * 2)
    A.mark()
    wo_buf = [A.bf(512 * 4), A.bf(512 * 4)]
    gg_blk = [A.bf(4096), A.bf(4096)]
    ctx.o = dict(wo_buf=wo_buf, gg_blk=gg_blk, mix_sb=mix_sb, x2_sb=x2_sb, xt2=xt2,
                 stats2=stats2, ffst=ffst)
    O = ctx.o

    gate = ctx.m_group_done_w + [(s["dve"], s["dve"].count),
                                 (s["mm"], s["mm"].count), (s["wgga"], s["wgga"].count),
                                 (s["wggb"], s["wggb"].count)]

    ww = [s["ww0"], s["ww1"]]
    wrd = [s["wrda"], s["wrdb"]]
    m_used = [s["mm"].count, s["mm"].count]
    m_evac = s["act"].count
    nmm = 0
    for o5 in range(8):
        for kb in range(16):
            par = nmm % 2
            ob, oh = o5 // 2, o5 % 2

            @ctx.op("gpsimd", waits=[(s["mm"], m_used[par])] + (gate if nmm < 2 else []),
                    inc=ww[par], inc_n=16)
            def _(e, O=O, ob=ob, oh=oh, kb=kb, par=par, ap=ap):
                srcv = ap["w_out_arr"][ob, 4 * kb:4 * kb + 4, :, 512 * oh:512 * (oh + 1)]
                return e.dma_start(out=O["wo_buf"][par].rearrange("p (k f) -> p k f", k=4),
                                   in_=srcv.rearrange("k p f -> p k f"))
            m_wo = ww[par].count

            @ctx.op("sync", waits=[(s["mm"], m_used[par])] + (gate if nmm < 2 else []),
                    inc=wrd[par], inc_n=16)
            def _(e, O=O, kb=kb, par=par, ap=ap):
                srcv = ap["gg_fm_d"][4 * kb:4 * kb + 4]
                return e.dma_start(out=O["gg_blk"][par].rearrange("p (kt f) -> p kt f", kt=32),
                                   in_=srcv.rearrange("k t p f -> p (k t) f"))
            m_gg = wrd[par].count

            @ctx.op("tensor", waits=[(ww[par], m_wo), (wrd[par], m_gg),
                                     (s["act"], m_evac)], inc=s["mm"])
            def _(e, O=O, kb=kb, par=par, pd=pd):
                gv = O["gg_blk"][par].rearrange("p (k t f) -> p k t f", k=4, t=8)
                wv = O["wo_buf"][par].rearrange("p (k f) -> p k f", k=4)
                last = None
                for kk in range(4):
                    for t in range(8):
                        last = e.matmul(pd[t // 2][:, 512 * (t % 2):512 * (t % 2 + 1)],
                                        gv[:, kk, t, :], wv[:, kk, :],
                                        start=(kb == 0 and kk == 0),
                                        stop=(kb == 15 and kk == 3))
                return last
            m_used[par] = s["mm"].count
            nmm += 1
        m_mmo = s["mm"].count

        @ctx.op("scalar", waits=[(s["mm"], m_mmo)])
        def _(e, O=O, o5=o5, pd=pd):
            last = None
            for t in range(8):
                last = e.activation(out=O["mix_sb"][:, D * t + 512 * o5:D * t + 512 * (o5 + 1)],
                                    in_=pd[t // 2][:, 512 * (t % 2):512 * (t % 2 + 1)],
                                    func=AF.Copy)
            return last
        m_evac = s["act"].count
    ctx.m_mix = m_evac
    ctx.m_op_end = dict(mm=s["mm"].count, wrda=s["wrda"].count, wrdb=s["wrdb"].count)
    A.release()


def _ln_stats_chain(ctx, src_tile, st, lnq, wait0):
    s = ctx.s

    @ctx.op("scalar", waits=wait0)
    def _(e, src_tile=src_tile, lnq=lnq):
        return e.activation(out=lnq, in_=src_tile, func=AF.Square)
    m_sq = s["act"].count

    @ctx.op("vector", waits=[(s["act"], m_sq)])
    def _(e, src_tile=src_tile, lnq=lnq, st=st):
        e.reduce_sum(out=st[:, 0:1], in_=src_tile, axis=AX.X)
        return e.reduce_sum(out=st[:, 1:2], in_=lnq, axis=AX.X)
    m_red = s["dve"].count

    @ctx.op("vector", waits=[(s["dve"], m_red)])
    def _(e, st=st):
        e.tensor_scalar(out=st[:, 2:3], in0=st[:, 0:1], scalar1=1.0 / D,
                        scalar2=None, op0=ALU.mult)
        return e.tensor_scalar(out=st[:, 3:4], in0=st[:, 1:2], scalar1=1.0 / D,
                               scalar2=None, op0=ALU.mult)
    m_sc = s["dve"].count

    @ctx.op("scalar", waits=[(s["dve"], m_sc)])
    def _(e, st=st):
        return e.activation(out=st[:, 5:6], in_=st[:, 2:3], func=AF.Square)
    m_m2 = s["act"].count

    @ctx.op("vector", waits=[(s["act"], m_m2)])
    def _(e, st=st):
        return e.tensor_sub(st[:, 3:4], st[:, 3:4], st[:, 5:6])
    m_var = s["dve"].count

    @ctx.op("scalar", waits=[(s["dve"], m_var)])
    def _(e, st=st, ctx=ctx):
        return e.activation(out=st[:, 6:7], in_=st[:, 3:4], func=AF.Ln,
                            bias=ctx.t["eps_col"][:, 0:1])
    m_ln = s["act"].count

    @ctx.op("scalar", waits=[(s["act"], m_ln)])
    def _(e, st=st):
        return e.activation(out=st[:, 4:5], in_=st[:, 6:7], func=AF.Exp, scale=-0.5)
    return s["act"].count


def plan_x2(ctx):
    s, ap, c = ctx.s, ctx.ap, ctx.c
    O = ctx.o
    lnq = ctx.t["ln_sq"]
    st = O["stats2"]
    m_tr = s["wtrh"].count
    for cx in range(NCH):
        mixc = O["mix_sb"][:, D * cx:D * (cx + 1)]
        x2c = O["x2_sb"][:, D * cx:D * (cx + 1)]

        @ctx.op("gpsimd", waits=[(s["dve"], s["dve"].count), (s["wtrh"], m_tr),
                                 (s["act"], s["act"].count)],
                inc=s["wx"], inc_n=16)
        def _(e, O=O, cx=cx, ap=ap):
            return e.dma_start(out=O["xt2"], in_=ap["xs"][CH * cx:CH * (cx + 1), :])
        m_x = s["wx"].count

        m_r = _ln_stats_chain(ctx, mixc, st, lnq,
                              [(s["act"], ctx.m_mix), (s["dve"], s["dve"].count)])

        @ctx.op("vector", waits=[(s["act"], m_r)])
        def _(e, mixc=mixc, st=st, lnq=lnq):
            return e.tensor_scalar(out=lnq, in0=mixc, scalar1=st[:, 2:3],
                                   scalar2=st[:, 4:5], op0=ALU.subtract, op1=ALU.mult)
        m_t1 = s["dve"].count

        @ctx.op("vector", waits=[(s["dve"], m_t1), (s["wx"], m_x)])
        def _(e, O=O, x2c=x2c, lnq=lnq):
            return e.tensor_add(x2c, lnq, O["xt2"])
        m_x2 = s["dve"].count

        m_r2 = _ln_stats_chain(ctx, x2c, st, lnq, [(s["dve"], m_x2)])

        h2st = O["xt2"].bitcast(BF16)[:, 0:D]

        @ctx.op("vector", waits=[(s["act"], m_r2), (s["wtrh"], m_tr)])
        def _(e, x2c=x2c, st=st, h2st=h2st):
            return e.tensor_scalar(out=h2st, in0=x2c, scalar1=st[:, 2:3],
                                   scalar2=st[:, 4:5], op0=ALU.subtract, op1=ALU.mult)
        m_h2 = s["dve"].count

        @ctx.op("sync", waits=[(s["dve"], m_h2)], inc=s["wtrh"], inc_n=16)
        def _(e, O=O, cx=cx, h2st=h2st):
            dst = O["mix_sb"][:, D * cx:D * (cx + 1)].rearrange("p (k t) -> p k t", k=KD)
            return e.dma_start(out=dst, in_=h2st.rearrange("p (k t) -> p k t", k=KD),
                               transpose=True)
        m_tr = s["wtrh"].count
    ctx.m_h2 = m_tr


def plan_ffn(ctx):
    s, ap, c = ctx.s, ctx.ap, ctx.c
    O = ctx.o
    A = ctx.A
    pd = ctx.pdub
    wgg = [s["wgga"], s["wggb"]]
    ww = [s["ww0"], s["ww1"]]
    NEGC = float(NEG)

    O["glw"] = A.bf(KD * 136)
    O["lbw"] = A.bf(D)
    O["fsq"] = A.f32(512)
    O["fstage"] = O["xt2"].bitcast(BF16)[:, 0:D]
    O["low_t"] = A.bf(128)
    O["low_fm"] = A.bf(NCH * 128)
    bb = [A.bf(4 * 512), A.bf(4 * 512)]
    fgate = [(s["mm"], ctx.m_op_end["mm"]), (s["act"], s["act"].count),
             (s["dve"], s["dve"].count), (s["wrda"], ctx.m_op_end["wrda"]),
             (s["wrdb"], ctx.m_op_end["wrdb"])]

    @ctx.op("gpsimd", waits=fgate, inc=s["wsp"], inc_n=16)
    def _(e, O=O, ap=ap):
        return e.dma_start(out=O["glw"].rearrange("p (k f) -> p k f", k=KD),
                           in_=ap["w_gl_arr"].rearrange("k p f -> p k f"))

    @ctx.op("gpsimd", waits=fgate, inc=s["wsp"], inc_n=16)
    def _(e, O=O, ap=ap):
        return e.dma_start(out=O["lbw"], in_=ap["w_lb_arr"])
    ctx.m_ffw = s["wsp"].count

    @ctx.op("vector", waits=fgate)
    def _(e, O=O):
        return e.memset(O["ffst"], 0.0)
    m_stz = s["dve"].count

    st = O["stats2"]
    for t in range(NCH):
        h2v = O["mix_sb"][:, D * t:D * (t + 1)].rearrange("p (k f) -> p k f", k=KD)

        @ctx.op("tensor", waits=[(s["wsp"], ctx.m_ffw), (s["wtrh"], ctx.m_h2),
                                 (s["act"], s["act"].count), (s["dve"], s["dve"].count)],
                inc=s["mm"])
        def _(e, O=O, h2v=h2v, pd=pd):
            glv = O["glw"].rearrange("p (k f) -> p k f", k=KD)
            last = None
            for k in range(KD):
                last = e.matmul(pd[2][:, 256:392], h2v[:, k, :], glv[:, k, :],
                                start=(k == 0), stop=(k == KD - 1))
            return last
        m_p1 = s["mm"].count

        lg = pd[2][:, 256:264]

        @ctx.op("vector", waits=[(s["mm"], m_p1)])
        def _(e, lg=lg, st=st):
            return e.reduce_max(out=st[:, 8:9], in_=lg, axis=AX.X)
        m1 = s["dve"].count

        @ctx.op("vector", waits=[(s["dve"], m1)])
        def _(e, lg=lg, st=st, O=O):
            return e.tensor_scalar(out=O["fsq"][:, 0:8], in0=lg, scalar1=st[:, 8:9],
                                   scalar2=None, op0=ALU.is_equal)
        m2 = s["dve"].count

        @ctx.op("vector", waits=[(s["dve"], m2)])
        def _(e, st=st, O=O, lg=lg, NEGC=NEGC):
            return e.scalar_tensor_tensor(out=O["fsq"][:, 8:16], in0=O["fsq"][:, 0:8],
                                          scalar=NEGC, in1=lg, op0=ALU.mult, op1=ALU.add)
        m3 = s["dve"].count

        @ctx.op("vector", waits=[(s["dve"], m3)])
        def _(e, st=st, O=O):
            e.reduce_max(out=st[:, 9:10], in_=O["fsq"][:, 8:16], axis=AX.X)
            return e.tensor_scalar(out=st[:, 10:11], in0=st[:, 8:9], scalar1=-1.0,
                                   scalar2=None, op0=ALU.mult)
        m4 = s["dve"].count

        @ctx.op("vector", waits=[(s["dve"], m4)])
        def _(e, lg=lg, st=st, O=O):
            return e.tensor_scalar(out=O["fsq"][:, 0:8], in0=lg, scalar1=st[:, 9:10],
                                   scalar2=None, op0=ALU.is_ge)
        m5 = s["dve"].count

        @ctx.op("scalar", waits=[(s["dve"], m5)])
        def _(e, lg=lg, st=st, O=O):
            e.activation(out=O["fsq"][:, 16:24], in_=lg, func=AF.Exp, bias=st[:, 10:11])
            return e.activation(out=st[:, 11:12], in_=st[:, 9:10], func=AF.Exp,
                                bias=st[:, 10:11])
        m6 = s["act"].count

        @ctx.op("vector", waits=[(s["act"], m6)])
        def _(e, st=st):
            return e.tensor_scalar(out=st[:, 12:13], in0=st[:, 11:12], scalar1=1.0,
                                   scalar2=None, op0=ALU.add)
        m7 = s["dve"].count

        @ctx.op("vector", waits=[(s["dve"], m7)])
        def _(e, st=st):
            return e.reciprocal(out=st[:, 13:14], in_=st[:, 12:13])
        m8 = s["dve"].count

        @ctx.op("vector", waits=[(s["dve"], m8)])
        def _(e, st=st, O=O):
            e.tensor_scalar(out=st[:, 14:15], in0=st[:, 13:14], scalar1=2.0,
                            scalar2=None, op0=ALU.mult)
            return e.tensor_mul(O["fsq"][:, 0:8], O["fsq"][:, 0:8], O["fsq"][:, 16:24])
        m9 = s["dve"].count

        @ctx.op("vector", waits=[(s["dve"], m9)])
        def _(e, st=st, O=O):
            return e.tensor_scalar(out=O["fsq"][:, 0:8], in0=O["fsq"][:, 0:8],
                                   scalar1=st[:, 14:15], scalar2=None, op0=ALU.mult)
        m10 = s["dve"].count

        @ctx.op("vector", waits=[(s["dve"], m10), (s["wgga"], s["wgga"].count),
                                 (s["wggb"], s["wggb"].count)])
        def _(e, O=O, pd=pd):
            wv = O["fsq"][:, 0:8].unsqueeze(2).broadcast_to([128, 8, 16])
            return e.tensor_tensor(out=O["low_t"].rearrange("p (x r) -> p x r", x=8),
                                   in0=pd[2][:, 264:392].rearrange("p (x r) -> p x r", x=8),
                                   in1=wv, op=ALU.mult)
        m_low = s["dve"].count

        par = t % 2

        @ctx.op("sync", waits=[(s["dve"], m_low), (wgg[par], wgg[par].count)],
                inc=wgg[par], inc_n=16)
        def _(e, O=O, t=t):
            return e.dma_start(out=O["low_fm"][:, 128 * t:128 * (t + 1)], in_=O["low_t"],
                               transpose=True)
        m_lowt = wgg[par].count
        ctx.m_ffn_lows = (s["dve"].count, wgg[0].count, wgg[1].count)
    m_lows = [(s["wgga"], s["wgga"].count), (s["wggb"], s["wggb"].count)]

    m_bused = [s["mm"].count, s["mm"].count]
    m_evB = s["act"].count
    m_stats = s["dve"].count
    nld = 0
    for o in range(8):
        for kb in range(8):
            par = nld % 2

            @ctx.op("gpsimd", waits=[(s["mm"], m_bused[par])], inc=ww[par], inc_n=16)
            def _(e, o=o, kb=kb, par=par, ap=ap, bb=bb):
                src = ap["w_base_arr"][4 * kb:4 * (kb + 1), :, 512 * o:512 * (o + 1)]
                return e.dma_start(out=bb[par].rearrange("p (k f) -> p k f", k=4),
                                   in_=src.rearrange("k p f -> p k f"))
            m_bw = ww[par].count

            @ctx.op("tensor", waits=[(ww[par], m_bw), (s["act"], m_evB)] + m_lows,
                    inc=s["mm"])
            def _(e, O=O, o=o, kb=kb, par=par, pd=pd, bb=bb):
                bv = bb[par].rearrange("p (k f) -> p k f", k=4)
                last = None
                for k in range(4):
                    kg = 4 * kb + k
                    for t in range(NCH):
                        h2v = O["mix_sb"][:, D * t + 128 * kg:D * t + 128 * (kg + 1)]
                        last = e.matmul(pd[t // 2][:, 512 * (t % 2):512 * (t % 2 + 1)],
                                        h2v, bv[:, k, :], start=(kg == 0), stop=False)
                return last
            m_bused[par] = s["mm"].count
            nld += 1

        @ctx.op("tensor", waits=[(s["mm"], m_bused[(nld - 1) % 2]),
                                 (s["wsp"], ctx.m_ffw)], inc=s["mm"])
        def _(e, O=O, c=c, o=o, pd=pd):
            last = None
            for t in range(NCH):
                e.matmul(pd[t // 2][:, 512 * (t % 2):512 * (t % 2 + 1)],
                         O["low_fm"][:, 128 * t:128 * (t + 1)],
                         O["lbw"][:, 512 * o:512 * (o + 1)], start=False, stop=False)
                last = e.matmul(pd[t // 2][:, 512 * (t % 2):512 * (t % 2 + 1)],
                                c["onesrow"][0:1, :], c["baseb"][0:1, 512 * o:512 * (o + 1)],
                                start=False, stop=True)
            return last
        m_f = s["mm"].count

        for t in range(NCH):
            @ctx.op("scalar", waits=[(s["mm"], m_f), (s["wsl"], s["wsl"].count),
                                     (s["dve"], m_stats)])
            def _(e, O=O, t=t, pd=pd):
                return e.activation(out=O["fstage"][:, 512 * t:512 * (t + 1)],
                                    in_=pd[t // 2][:, 512 * (t % 2):512 * (t % 2 + 1)],
                                    func=AF.Copy)
        m_ev = s["act"].count

        for t in range(NCH):
            @ctx.op("scalar", waits=[(s["act"], m_ev), (s["dve"], m_stats)])
            def _(e, O=O, t=t):
                return e.activation(out=O["fsq"][:, 0:512],
                                    in_=O["fstage"][:, 512 * t:512 * (t + 1)],
                                    func=AF.Square)
            m_q = s["act"].count

            @ctx.op("vector", waits=[(s["act"], m_q), (s["dve"], m_stz)])
            def _(e, O=O, t=t):
                e.reduce_sum(out=st[:, 8:9], in_=O["fstage"][:, 512 * t:512 * (t + 1)],
                             axis=AX.X)
                return e.reduce_sum(out=st[:, 9:10], in_=O["fsq"][:, 0:512], axis=AX.X)
            m_rd = s["dve"].count

            @ctx.op("vector", waits=[(s["dve"], m_rd)])
            def _(e, O=O, t=t, st=st):
                e.tensor_add(O["ffst"][:, 2 * t:2 * t + 1], O["ffst"][:, 2 * t:2 * t + 1],
                             st[:, 8:9])
                return e.tensor_add(O["ffst"][:, 2 * t + 1:2 * t + 2],
                                    O["ffst"][:, 2 * t + 1:2 * t + 2], st[:, 9:10])
            m_stats = s["dve"].count

        @ctx.op("sync", waits=[(s["act"], m_ev)], inc=s["wsl"], inc_n=16)
        def _(e, O=O, o=o, ap=ap):
            return e.dma_start(out=ap["ffn_d"][:, :, 512 * o:512 * (o + 1)].rearrange("t p f -> p t f"),
                               in_=O["fstage"].rearrange("p (t f) -> p t f", t=NCH))
        m_evB = s["act"].count
    ctx.m_ffn = dict(dve=s["dve"].count, act=s["act"].count, wsl=s["wsl"].count)


def plan_x3(ctx):
    s, ap, c = ctx.s, ctx.ap, ctx.c
    O = ctx.o
    st = O["stats2"]
    ffnst = O["mix_sb"][:, 0:D]
    t1 = O["mix_sb"][:, D:3 * D].bitcast(F32)
    x2cp = O["mix_sb"][:, 3 * D:5 * D].bitcast(F32)
    m_prev = s["dve"].count
    for cx in range(NCH):
        @ctx.op("sync", waits=[(s["wsl"], ctx.m_ffn["wsl"]), (s["act"], s["act"].count),
                               (s["dve"], m_prev), (s["wout"], s["wout"].count),
                               (s["mm"], s["mm"].count)],
                inc=s["wrha"], inc_n=16)
        def _(e, ffnst=ffnst, cx=cx, ap=ap):
            return e.dma_start(out=ffnst, in_=ap["ffn_d"][cx])
        m_f = s["wrha"].count

        @ctx.op("vector", waits=[(s["dve"], ctx.m_ffn["dve"]), (s["act"], s["act"].count)])
        def _(e, O=O, cx=cx, st=st):
            e.tensor_scalar(out=st[:, 2:3], in0=O["ffst"][:, 2 * cx:2 * cx + 1],
                            scalar1=1.0 / D, scalar2=None, op0=ALU.mult)
            return e.tensor_scalar(out=st[:, 3:4], in0=O["ffst"][:, 2 * cx + 1:2 * cx + 2],
                                   scalar1=1.0 / D, scalar2=None, op0=ALU.mult)
        m_s = s["dve"].count

        @ctx.op("scalar", waits=[(s["dve"], m_s)])
        def _(e, st=st):
            return e.activation(out=st[:, 5:6], in_=st[:, 2:3], func=AF.Square)
        m_s2 = s["act"].count

        @ctx.op("vector", waits=[(s["act"], m_s2)])
        def _(e, st=st):
            return e.tensor_sub(st[:, 3:4], st[:, 3:4], st[:, 5:6])
        m_v = s["dve"].count

        @ctx.op("scalar", waits=[(s["dve"], m_v)])
        def _(e, st=st):
            return e.activation(out=st[:, 6:7], in_=st[:, 3:4], func=AF.Ln,
                                bias=ctx.t["eps_col"][:, 0:1])
        m_l = s["act"].count

        @ctx.op("scalar", waits=[(s["act"], m_l)])
        def _(e, st=st):
            return e.activation(out=st[:, 4:5], in_=st[:, 6:7], func=AF.Exp, scale=-0.5)
        m_r = s["act"].count

        @ctx.op("vector", waits=[(s["act"], m_r), (s["wrha"], m_f)])
        def _(e, ffnst=ffnst, t1=t1, st=st):
            return e.tensor_scalar(out=t1, in0=ffnst, scalar1=st[:, 2:3],
                                   scalar2=st[:, 4:5], op0=ALU.subtract, op1=ALU.mult)
        m_t1 = s["dve"].count

        @ctx.op("vector", waits=[(s["dve"], m_t1)])
        def _(e, O=O, cx=cx, x2cp=x2cp):
            return e.tensor_copy(out=x2cp, in_=O["x2_sb"][:, D * cx:D * (cx + 1)])
        m_c = s["dve"].count

        @ctx.op("vector", waits=[(s["dve"], m_c)])
        def _(e, t1=t1, x2cp=x2cp):
            return e.tensor_add(t1, t1, x2cp)
        m_o = s["dve"].count

        @ctx.op("sync", waits=[(s["dve"], m_o)], inc=s["wout"], inc_n=16)
        def _(e, t1=t1, cx=cx, ap=ap):
            return e.dma_start(out=ap["ys"][CH * cx:CH * (cx + 1), :], in_=t1)
        m_prev = m_o


def plan_post_dbg(ctx):
    s = ctx.s
    allw = [(sm, sm.count) for sm in ctx.plan.sems if sm.count]
    if hasattr(ctx, "o"):
        ctx.dbg_src["x2_o"] = (ctx.o["x2_sb"], allw)
    ctx.dbg_src["ggd_o"] = (ctx.ap["gg_fm_d"], allw)
    ctx.dbg_src["ffnd_o"] = (ctx.ap["ffn_d"], allw)
    ctx.dbg_src["sld_o"] = (ctx.ap["sinloc_d"], allw)


def plan_debug(ctx, dbg):
    s = ctx.s
    for name, shape, dt in dbg:
        src, waits = ctx.dbg_src[name]

        @ctx.op("sync", waits=waits, inc=s["wout"], inc_n=16)
        def _(e, name=name, src=src):
            return e.dma_start(out=ctx.ap[name], in_=src)


def plan_finish(ctx):
    s = ctx.s

    @ctx.op("sync", waits=[(sm, sm.count) for sm in ctx.plan.sems if sm.count])
    def _(e):
        return None



def kernel(**inputs):
    in_maps = host_prepare(**inputs)
    nc = build_program(phases=("H", "G", "SSD", "OP", "X2", "FFN", "X3"))
    res = run_bass_kernel_spmd(nc, in_maps, core_ids=list(range(NC)))
    out = np.concatenate([res.results[i]["ys"] for i in range(NC)], axis=0)
    return out.reshape(1, NC * T, D).astype(np.float32)
